# version 14
# baseline (speedup 1.0000x reference)
import os
import numpy as np
import ml_dtypes
import concourse.bass as bass
import concourse.mybir as mybir
from concourse.bass_utils import run_bass_kernel_spmd
from contextlib import ExitStack

F32 = mybir.dt.float32
BF16 = mybir.dt.bfloat16
I32 = mybir.dt.int32
AF = mybir.ActivationFunctionType
ALU = mybir.AluOpType
AX = mybir.AxisListType
BF = ml_dtypes.bfloat16

SEQ = 2
L = 2048
D = 2048
NH = 8
HD = 256
I = 5632
TT = 128
NTS = L // TT
NC = 16
NCI = I // 128
TB = 512
NB = SEQ * L // TB
NTB = TB // TT
NJ = I // 512
TL = SEQ * L
ENGS = ("sync", "tensor", "vector", "scalar", "gpsimd")
NEGBIG = -30000.0
ALLPH = ("A0", "W", "A1", "A2", "B", "C")


class Sem:
    def __init__(self, es, nc, name, queue=None):
        self.h = es.enter_context(nc.semaphore(name))
        self.n = 0
        self.queue = queue


DRAIN_ENGS = ("vector", "scalar", "gpsimd")


class Ph:
    def __init__(self, nc):
        self.nc = nc
        self.ops = {e: [] for e in ENGS}
        self.dma_sems = []
        self.drained = {e: True for e in ENGS}

    def raw(self, eng, fn):
        if eng in DRAIN_ENGS and not self.drained[eng]:
            self.ops[eng].append(lambda e: e.drain())
        self.drained[eng] = False
        self.ops[eng].append(fn)

    def op(self, eng, fn, inc=None):
        self.raw(eng, fn)
        if inc is None:
            return None
        sem, k = inc
        sem.n += k
        self.ops[eng].append(lambda e, h=sem.h, k=k: e.drain().then_inc(h, k))
        self.drained[eng] = True
        return sem.n

    def wait(self, eng, sem, val=None):
        v = sem.n if val is None else val
        if v > 0:
            self.ops[eng].append(lambda e, h=sem.h, v=v: e.wait_ge(h, v))

    def dma(self, q, dst, src, sem, transpose=False):
        assert sem.queue == q
        if sem not in self.dma_sems:
            self.dma_sems.append(sem)
        sem.n += 16
        if transpose:
            self.ops[q].append(lambda e, d=dst, s=src, h=sem.h:
                               e.dma_start(d, s, transpose=True).then_inc(h, 16))
        else:
            self.ops[q].append(lambda e, d=dst, s=src, h=sem.h:
                               e.dma_start(d, s).then_inc(h, 16))
        return sem.n

    def mark(self, eng, sem):
        sem.n += 1
        self.ops[eng].append(lambda e, h=sem.h: e.drain().then_inc(h, 1))
        self.drained[eng] = True
        return sem.n

    def inc_last(self, eng, sem, k=1):
        sem.n += k
        old = self.ops[eng][-1]
        self.ops[eng][-1] = lambda e, old=old, h=sem.h, k=k: old(e).then_inc(h, k)
        return sem.n

    def flush(self):
        for sem in self.dma_sems:
            self.wait(sem.queue, sem)
        self.dma_sems = []
        with self.nc.Block() as blk:
            for e in ENGS:
                lst = self.ops[e]
                if lst:
                    getattr(blk, e)(lambda eng, lst=lst: [f(eng) for f in lst])
        self.ops = {e: [] for e in ENGS}
        self.drained = {e: True for e in ENGS}


def build(debug=False, phases=ALLPH):
    nc = bass.Bass("TRN2", target_bir_lowering=False)

    x_in = nc.dram_tensor("x_in", [TL, D], F32, kind="ExternalInput").ap()
    qTw_h = nc.dram_tensor("qTw", [D, D], F32, kind="ExternalInput").ap()
    kTw_h = nc.dram_tensor("kTw", [D, HD], F32, kind="ExternalInput").ap()
    vTw_h = nc.dram_tensor("vTw", [D, HD], F32, kind="ExternalInput").ap()
    oTw_h = nc.dram_tensor("oTw", [D, D], F32, kind="ExternalInput").ap()
    w1_h = nc.dram_tensor("w1n", [2 * I, D], F32, kind="ExternalInput").ap()
    w2_h = nc.dram_tensor("w2n", [D, I], F32, kind="ExternalInput").ap()
    qnb_h = nc.dram_tensor("qnb", [128, D], F32, kind="ExternalInput").ap()
    knb_h = nc.dram_tensor("knb", [128, HD], F32, kind="ExternalInput").ap()
    ctab_h = nc.dram_tensor("ctab", [L, HD], F32, kind="ExternalInput").ap()
    stab_h = nc.dram_tensor("stab", [L, HD], F32, kind="ExternalInput").ap()
    mskd_h = nc.dram_tensor("mskd", [128, 128], F32, kind="ExternalInput").ap()
    id16_h = nc.dram_tensor("id16", [128, 128], BF16, kind="ExternalInput").ap()
    cst_h = nc.dram_tensor("cst", [128, 8], F32, kind="ExternalInput").ap()
    y_out = nc.dram_tensor("y", [TL, D], F32, kind="ExternalOutput").ap()

    dbg = "ExternalOutput" if debug else "Internal"
    xbf_d = nc.dram_tensor("xbf_d", [TL, D], BF16, kind=dbg).ap()
    qbf_d = nc.dram_tensor("qbf_d", [TL, D], BF16, kind=dbg).ap()
    kbf_d = nc.dram_tensor("kbf_d", [TL, HD], BF16, kind=dbg).ap()
    x1_d = nc.dram_tensor("x1_d", [TL, D], BF16, kind=dbg).ap()
    qx_d = nc.dram_tensor("qx_d", [TL, D], BF16, kind=dbg).ap()
    h_d = nc.dram_tensor("h_d", [TL, I], BF16, kind=dbg).ap()
    qw2R_d = nc.dram_tensor("qw2R_d", [D, I], BF16, kind=dbg).ap()
    accT_d = nc.dram_tensor("accT_d", [D, TL], BF16, kind=dbg).ap()

    es = ExitStack()
    with es:
        sPE = Sem(es, nc, "sPE")
        sDV = Sem(es, nc, "sDV")
        sAC = Sem(es, nc, "sAC")
        sGP = Sem(es, nc, "sGP")

        id16 = es.enter_context(nc.sbuf_tensor("id16s", [128, 128], BF16))
        mskd = es.enter_context(nc.sbuf_tensor("mskds", [128, 128], F32))
        sca = es.enter_context(nc.sbuf_tensor("sca", [128, 64], F32))
        cst = es.enter_context(nc.sbuf_tensor("cst_s", [128, 8], F32))

        def CST(c):
            return cst[:, c:c + 1]

        def SC(c, n=1):
            return sca[:, c:c + n]

        ph = Ph(nc)

        sLc0 = Sem(es, nc, "sLc0", "sync")
        ph.dma("sync", id16[:, :], id16_h[:, :], sLc0)
        ph.dma("sync", mskd[:, :], mskd_h[:, :], sLc0)
        ph.dma("sync", cst[:, :], cst_h[:, :], sLc0)
        consts_ready = sLc0.n
        a0_sems = []
        if "A0" in phases:
            sLa = Sem(es, nc, "sLa", "gpsimd")
            a0_sems = [sLa]
            for it in range(TL // TT):
                ph.dma("gpsimd", xbf_d[it * TT:(it + 1) * TT, :],
                       x_in[it * TT:(it + 1) * TT, :], sLa)

        esAT = ExitStack()
        sqk_done = {}
        sSqk = None
        sx_done = {}
        if "A1" in phases or "A2" in phases:
            v_sb = esAT.enter_context(nc.sbuf_tensor("v_sb", [128, SEQ * NTS * HD], BF16))

        if "A1" in phases:
            esA1 = ExitStack()
            qTw = esA1.enter_context(nc.sbuf_tensor("qTwS", [128, NC * D], BF16))
            kTw = esA1.enter_context(nc.sbuf_tensor("kTwS", [128, NC * HD], BF16))
            vTw = esA1.enter_context(nc.sbuf_tensor("vTwS", [128, NC * HD], BF16))
            xT = [esA1.enter_context(nc.sbuf_tensor(f"xT{b}", [128, NC * 512], BF16))
                  for b in range(2)]
            qnb = esA1.enter_context(nc.sbuf_tensor("qnbS", [128, D], F32))
            knb = esA1.enter_context(nc.sbuf_tensor("knbS", [128, HD], F32))
            ctab = [esA1.enter_context(nc.sbuf_tensor(f"ctab{b}", [128, HD], F32))
                    for b in range(2)]
            stab = [esA1.enter_context(nc.sbuf_tensor(f"stab{b}", [128, HD], F32))
                    for b in range(2)]
            qhat = esA1.enter_context(nc.sbuf_tensor("qhat", [128, D], F32))
            rt1 = esA1.enter_context(nc.sbuf_tensor("rt1", [128, D], F32))
            rt2 = esA1.enter_context(nc.sbuf_tensor("rt2", [128, D], F32))
            qbf = esA1.enter_context(nc.sbuf_tensor("qbf", [128, D], BF16))
            khat = esA1.enter_context(nc.sbuf_tensor("khat", [128, HD], F32))
            kr1 = esA1.enter_context(nc.sbuf_tensor("kr1", [128, HD], F32))
            kr2 = esA1.enter_context(nc.sbuf_tensor("kr2", [128, HD], F32))
            kbf = esA1.enter_context(nc.sbuf_tensor("kbf", [128, HD], BF16))
            sqg = esA1.enter_context(nc.sbuf_tensor("sqg", [128, HD], F32))
            qps = esA1.enter_context(nc.psum_tensor("qps", [128, D], F32))
            kps = esA1.enter_context(nc.psum_tensor("kps", [128, 512], F32))
            vps = esA1.enter_context(nc.psum_tensor("vps", [128, 512], F32))
            sLwq = Sem(es, nc, "sLwq", "gpsimd")
            sLnb = Sem(es, nc, "sLnb", "sync")
            sLxT = [Sem(es, nc, f"sLxT{b}", "sync") for b in range(2)]
            sLtab = [Sem(es, nc, f"sLtab{b}", "scalar") for b in range(2)]
            sSqk = Sem(es, nc, "sSqk", "scalar")

            for c in range(NC):
                ph.dma("gpsimd", qTw[:, c * D:(c + 1) * D],
                       qTw_h[c * 128:(c + 1) * 128, :], sLwq)
                ph.dma("gpsimd", kTw[:, c * HD:(c + 1) * HD],
                       kTw_h[c * 128:(c + 1) * 128, :], sLwq)
                ph.dma("gpsimd", vTw[:, c * HD:(c + 1) * HD],
                       vTw_h[c * 128:(c + 1) * 128, :], sLwq)
            wload = sLwq.n
            ph.dma("sync", qnb[:, :], qnb_h[:, :], sLnb)
            ph.dma("sync", knb[:, :], knb_h[:, :], sLnb)
            nb_ready = sLnb.n

            NQ = 4
            pe_tile_done = {}
            psum_free = {}
            gp_done = {}
            dv_rope_done = {}
            for s_ in a0_sems:
                ph.wait("sync", s_)
            for qb_i in range(TL // (NQ * TT)):
                b = qb_i % 2
                sq = qb_i // 4
                q_in_seq = qb_i % 4
                if qb_i >= 2:
                    ph.wait("sync", sPE, pe_tile_done.get((qb_i - 2, NQ - 1), 0))
                for c in range(NC):
                    ph.dma("sync", xT[b][:, c * 512:(c + 1) * 512],
                           xbf_d[sq * L + q_in_seq * 512: sq * L + (q_in_seq + 1) * 512,
                                 c * 128:(c + 1) * 128], sLxT[b], transpose=True)
                xt_ready = sLxT[b].n
                for ti in range(NQ):
                    it = qb_i * NQ + ti
                    trow = it * TT
                    tb_ = it % 2
                    if it >= 2:
                        ph.wait("scalar", sGP, gp_done.get(it - 2, 0))
                        ph.wait("scalar", sDV, dv_rope_done.get(it - 2, 0))
                    ph.dma("scalar", ctab[tb_][:, :],
                           ctab_h[(it % NTS) * TT:((it % NTS) + 1) * TT, :], sLtab[tb_])
                    ph.dma("scalar", stab[tb_][:, :],
                           stab_h[(it % NTS) * TT:((it % NTS) + 1) * TT, :], sLtab[tb_])
                    tabs_ready = sLtab[tb_].n
                    ph.wait("tensor", sLwq, wload)
                    ph.wait("tensor", sLxT[b], xt_ready)
                    if it >= 1:
                        ph.wait("tensor", sDV, psum_free[it - 1])
                    for c in range(NC):
                        lhs = xT[b][:, c * 512 + ti * 128: c * 512 + (ti + 1) * 128]
                        st, sp = (c == 0), (c == NC - 1)
                        for ob in range(4):
                            ph.raw("tensor", lambda e, l=lhs, ob=ob, st=st, sp=sp, c=c:
                                   e.matmul(qps[:, ob * 512:(ob + 1) * 512], l,
                                            qTw[:, c * D + ob * 512: c * D + (ob + 1) * 512],
                                            start=st, stop=sp))
                        ph.raw("tensor", lambda e, l=lhs, st=st, sp=sp, c=c: e.matmul(
                            kps[:, 0:HD], l, kTw[:, c * HD:(c + 1) * HD],
                            start=st, stop=sp))
                        ph.raw("tensor", lambda e, l=lhs, st=st, sp=sp, c=c: e.matmul(
                            vps[:, 0:HD], l, vTw[:, c * HD:(c + 1) * HD],
                            start=st, stop=sp))
                    pe_tile_done[(qb_i, ti)] = ph.inc_last("tensor", sPE)
                    ph.wait("scalar", sPE, pe_tile_done[(qb_i, ti)])
                    for h in range(NH):
                        ph.raw("scalar", lambda e, h=h: e.activation(
                            sqg[:, :], qps[:, h * HD:(h + 1) * HD], AF.Square,
                            accum_out=SC(12 + h)))
                    ph.raw("scalar", lambda e: e.activation(
                        sqg[:, :], kps[:, 0:HD], AF.Square, accum_out=SC(20)))
                    ph.wait("scalar", sLc0, consts_ready)
                    ph.raw("scalar", lambda e: e.activation(
                        SC(21, 9), SC(12, 9), AF.Ln, scale=1.0 / 256.0, bias=CST(0)))
                    ph.raw("scalar", lambda e: e.activation(
                        SC(21, 9), SC(21, 9), AF.Identity, scale=-0.5))
                    ac_rms = ph.op("scalar", lambda e: e.activation(
                        SC(21, 9), SC(21, 9), AF.Exp), (sAC, 1))
                    ph.wait("vector", sAC, ac_rms)
                    ph.wait("vector", sLnb, nb_ready)
                    for h in range(NH):
                        ph.raw("vector", lambda e, h=h: e.scalar_tensor_tensor(
                            qhat[:, h * HD:(h + 1) * HD], qps[:, h * HD:(h + 1) * HD],
                            SC(21 + h), qnb[:, h * HD:(h + 1) * HD], ALU.mult, ALU.mult))
                    ph.raw("vector", lambda e: e.scalar_tensor_tensor(
                        khat[:, :], kps[:, 0:HD], SC(29), knb[:, :], ALU.mult, ALU.mult))
                    psum_free[it] = ph.op("vector", lambda e, it=it: e.tensor_copy(
                        v_sb[:, it * HD:(it + 1) * HD], vps[:, 0:HD]), (sDV, 1))
                    ct, st_ = ctab[tb_], stab[tb_]
                    ph.wait("gpsimd", sDV, psum_free[it])
                    ph.wait("gpsimd", sLtab[tb_], tabs_ready)
                    ph.wait("vector", sLtab[tb_], tabs_ready)
                    for h in range(NH):
                        q0 = h * HD
                        ph.raw("vector", lambda e, q0=q0, ct=ct: e.tensor_tensor(
                            rt1[:, q0:q0 + HD], qhat[:, q0:q0 + HD], ct[:, :], ALU.mult))
                        ph.raw("gpsimd", lambda e, q0=q0, st_=st_: e.tensor_tensor(
                            rt2[:, q0:q0 + 128], qhat[:, q0 + 128:q0 + HD],
                            st_[:, 0:128], ALU.mult))
                        ph.raw("gpsimd", lambda e, q0=q0, st_=st_: e.tensor_tensor(
                            rt2[:, q0 + 128:q0 + HD], qhat[:, q0:q0 + 128],
                            st_[:, 128:HD], ALU.mult))
                    gp_done[it] = ph.mark("gpsimd", sGP)
                    ph.wait("vector", sGP, gp_done[it])
                    ph.wait("vector", sSqk)
                    ph.raw("vector", lambda e: e.tensor_tensor(
                        qbf[:, :], rt1[:, :], rt2[:, :], ALU.add))
                    ph.raw("vector", lambda e, ct=ct: e.tensor_tensor(
                        kr1[:, :], khat[:, :], ct[:, :], ALU.mult))
                    ph.raw("vector", lambda e, st_=st_: e.tensor_tensor(
                        kr2[:, 0:128], khat[:, 128:HD], st_[:, 0:128], ALU.mult))
                    ph.raw("vector", lambda e, st_=st_: e.tensor_tensor(
                        kr2[:, 128:HD], khat[:, 0:128], st_[:, 128:HD], ALU.mult))
                    dv_rope_done[it] = ph.op("vector", lambda e: e.tensor_tensor(
                        kbf[:, :], kr1[:, :], kr2[:, :], ALU.add), (sDV, 1))
                    ph.wait("scalar", sDV, dv_rope_done[it])
                    ph.dma("scalar", qbf_d[trow:trow + TT, :], qbf[:, :], sSqk)
                    ph.dma("scalar", kbf_d[trow:trow + TT, :], kbf[:, :], sSqk)
                    sqk_done[it] = sSqk.n
            ph.flush()
            esA1.close()

        if "A2" in phases:
            esA2 = ExitStack()
            oTw = esA2.enter_context(nc.sbuf_tensor("oTwS", [128, NC * D], BF16))
            qT = esA2.enter_context(nc.sbuf_tensor("qTS", [128, NC * 512], BF16))
            kT = esA2.enter_context(nc.sbuf_tensor("kTS", [128, 2 * L], BF16))
            w_sb = [esA2.enter_context(nc.sbuf_tensor(f"w_sb{b}", [128, 2048], BF16))
                    for b in range(2)]
            wT_all = esA2.enter_context(
                nc.sbuf_tensor("wT_all", [128, NH * 2048], BF16))
            ctxT = esA2.enter_context(nc.sbuf_tensor("ctxT", [128, D], BF16))
            x_sb = [esA2.enter_context(nc.sbuf_tensor(f"x_sb{b}", [128, D], F32))
                    for b in range(2)]
            y_sb = esA2.enter_context(nc.sbuf_tensor("y_sb", [128, D], F32))
            i32a = esA2.enter_context(nc.sbuf_tensor("i32a", [128, D], I32))
            qx_sb = esA2.enter_context(nc.sbuf_tensor("qx_sb", [128, D], BF16))
            sqga = esA2.enter_context(nc.sbuf_tensor("sqga", [128, D], F32))
            scp = [esA2.enter_context(nc.psum_tensor(f"scp{b}", [128, 1024], F32))
                   for b in range(2)]
            psw = [esA2.enter_context(nc.psum_tensor(f"psw{b}", [128, 1024], BF16))
                   for b in range(2)]
            cop = esA2.enter_context(nc.psum_tensor("cop", [128, 1024], F32))
            sLo = Sem(es, nc, "sLo", "gpsimd")
            sLkT = Sem(es, nc, "sLkT", "sync")
            sLqT = Sem(es, nc, "sLqT", "sync")
            sLx = [Sem(es, nc, f"sLx{b}", "sync") for b in range(2)]
            sSx = Sem(es, nc, "sSx", "scalar")
            sSx1 = Sem(es, nc, "sSx1", "gpsimd")

            for c in range(NC):
                ph.dma("gpsimd", oTw[:, c * D:(c + 1) * D],
                       oTw_h[c * 128:(c + 1) * 128, :], sLo)
            oload = sLo.n

            scp_free = {0: 0, 1: 0}
            psw_ev = {0: 0, 1: 0}
            tr_done = {}
            norm_done = {}
            pend_tr = []
            last_sc_pm = 0
            kT_free = 0
            qT_free = 0
            xfree = {}
            cop_free = 0
            pswc = 0

            def emit_tr(gh, h, nch, Lk):
                nonlocal pswc
                wpar = gh % 2
                lastpm = 0
                for j0 in range(0, nch, 8):
                    j1 = min(j0 + 8, nch)
                    gp = pswc % 2
                    pswc += 1
                    ph.wait("tensor", sGP, norm_done[gh])
                    ph.wait("tensor", sDV, psw_ev[gp])
                    for j in range(j0, j1):
                        ph.raw("tensor", lambda e, gp=gp, j=j, j0=j0, wpar=wpar:
                               e.transpose(
                            psw[gp][:, (j - j0) * 128:(j - j0 + 1) * 128],
                            w_sb[wpar][:, j * 128:(j + 1) * 128], id16[:, :]))
                    lastpm = ph.inc_last("tensor", sPE)
                    ph.wait("vector", sPE, lastpm)
                    psw_ev[gp] = ph.op(
                        "vector", lambda e, gp=gp, h=h, j0=j0, j1=j1: e.tensor_copy(
                            wT_all[:, h * 2048 + j0 * 128: h * 2048 + j1 * 128],
                            psw[gp][:, 0:(j1 - j0) * 128]), (sDV, 1))
                tr_done[gh] = lastpm

            for sq in range(SEQ):
                if sqk_done:
                    ph.wait("sync", sSqk, sqk_done[sq * NTS + NTS - 1])
                if sq > 0:
                    ph.wait("sync", sPE, kT_free)
                for c2 in range(2):
                    ph.dma("sync", kT[:, c2 * L:(c2 + 1) * L],
                           kbf_d[sq * L:(sq + 1) * L, c2 * 128:(c2 + 1) * 128], sLkT,
                           transpose=True)
                kt_ready = sLkT.n
                for qh in range(4):
                    if sq + qh > 0:
                        ph.wait("sync", sPE, qT_free)
                    for c in range(NC):
                        ph.dma("sync", qT[:, c * 512:(c + 1) * 512],
                               qbf_d[sq * L + qh * 512: sq * L + (qh + 1) * 512,
                                     c * 128:(c + 1) * 128], sLqT, transpose=True)
                    qt_ready = sLqT.n
                    for ti in range(4):
                        i = qh * 4 + ti
                        it = sq * NTS + i
                        trow = it * TT
                        nch = i + 1
                        Lk = nch * 128
                        npt = 1 if nch <= 8 else 2
                        xb = it % 2
                        if it >= 2:
                            ph.wait("sync", sDV, xfree.get(it - 2, 0))
                        ph.dma("sync", x_sb[xb][:, :], x_in[trow:trow + TT, :],
                               sLx[xb])
                        x_ready = sLx[xb].n
                        for h in range(NH):
                            gh = it * NH + h
                            wpar = gh % 2
                            B0 = 12 + 8 * (gh % 2)
                            ph.wait("vector", sGP, norm_done.get(gh - 2, 0))
                            qts = [qT[:, c_ * 512 + ti * 128:
                                      c_ * 512 + (ti + 1) * 128]
                                   for c_ in (2 * h, 2 * h + 1)]
                            if h == 0 and ti == 0:
                                ph.wait("tensor", sLqT, qt_ready)
                                ph.wait("tensor", sLkT, kt_ready)
                            pids = []
                            for p in range(npt):
                                pidx = (gh % 2) if npt == 1 else p
                                pids.append(pidx)
                                pw = min(8, nch - p * 8)
                                ph.wait("tensor", sAC, scp_free[pidx])
                                nsg = (pw * 128 + 511) // 512
                                for c2 in range(2):
                                    for sg in range(nsg):
                                        s0 = sg * 512
                                        sw = min(512, pw * 128 - s0)
                                        ph.raw("tensor", lambda e, pidx=pidx,
                                               s0=s0, sw=sw, c2=c2, p=p, q_=qts[c2]:
                                               e.matmul(
                                            scp[pidx][:, s0:s0 + sw], q_,
                                            kT[:, c2 * L + p * 1024 + s0:
                                               c2 * L + p * 1024 + s0 + sw],
                                            start=(c2 == 0), stop=(c2 == 1)))
                                pm_sc = ph.inc_last("tensor", sPE)
                                last_sc_pm = pm_sc
                                ph.wait("vector", sPE, pm_sc)
                                if p == npt - 1:
                                    dcol = (nch - 1 - p * 8) * 128
                                    ph.raw("vector", lambda e, pidx=pidx, dcol=dcol:
                                           e.tensor_tensor(
                                        scp[pidx][:, dcol:dcol + 128],
                                        scp[pidx][:, dcol:dcol + 128],
                                        mskd[:, :], ALU.add))
                                ph.raw("vector", lambda e, pidx=pidx, pw=pw,
                                       B0=B0, p=p: e.reduce_max(
                                    SC(B0 + p), scp[pidx][:, 0:pw * 128],
                                    axis=AX.X, negate=True))
                            if npt == 2:
                                ph.raw("vector", lambda e, B0=B0: e.tensor_tensor(
                                    SC(B0 + 2), SC(B0), SC(B0 + 1), ALU.min))
                                bsrc = B0 + 2
                            else:
                                bsrc = B0
                            dv_b = ph.op("vector", lambda e, B0=B0, bsrc=bsrc:
                                         e.tensor_scalar(
                                SC(B0 + 3), SC(bsrc), 1.0 / 16.0, None, ALU.mult),
                                (sDV, 1))
                            if pend_tr:
                                emit_tr(*pend_tr.pop(0))
                            ph.wait("scalar", sDV, dv_b)
                            ph.wait("scalar", sPE, tr_done.get(gh - 2, 0))
                            for p in range(npt):
                                pidx = pids[p]
                                pw = min(8, nch - p * 8)
                                ph.raw("scalar", lambda e, pidx=pidx, pw=pw, p=p,
                                       wpar=wpar, B0=B0: e.activation(
                                    w_sb[wpar][:, p * 1024: p * 1024 + pw * 128],
                                    scp[pidx][:, 0:pw * 128], AF.Exp,
                                    bias=SC(B0 + 3), scale=1.0 / 16.0,
                                    accum_out=SC(B0 + 4 + p)))
                            ed = ph.inc_last("scalar", sAC)
                            for p in range(npt):
                                scp_free[pids[p]] = ed
                            ph.wait("vector", sAC, ed)
                            if npt == 2:
                                ph.raw("vector", lambda e, B0=B0: e.tensor_tensor(
                                    SC(B0 + 6), SC(B0 + 4), SC(B0 + 5), ALU.add))
                                ssrc = B0 + 6
                            else:
                                ssrc = B0 + 4
                            rec_done = ph.op("vector", lambda e, B0=B0, ssrc=ssrc:
                                             e.reciprocal(SC(B0 + 7), SC(ssrc)),
                                             (sDV, 1))
                            ph.wait("gpsimd", sDV, rec_done)
                            norm_done[gh] = ph.op(
                                "gpsimd", lambda e, wpar=wpar, Lk=Lk, B0=B0:
                                e.tensor_scalar(
                                    w_sb[wpar][:, 0:Lk], w_sb[wpar][:, 0:Lk],
                                    SC(B0 + 7), None, ALU.mult), (sGP, 1))
                            pend_tr.append((gh, h, nch, Lk))
                        while pend_tr:
                            emit_tr(*pend_tr.pop(0))
                        ph.wait("tensor", sDV, psw_ev[0])
                        ph.wait("tensor", sDV, psw_ev[1])
                        ph.wait("tensor", sDV, cop_free)
                        for half in range(2):
                            for j in range(nch):
                                lhs = v_sb[:, (sq * NTS + j) * HD + half * 128:
                                           (sq * NTS + j) * HD + half * 128 + 128]
                                for sg in range(2):
                                    rhs = bass.AP(wT_all, (sg * 4) * 2048 + j * 128,
                                                  [[NH * 2048, 128], [2048, 4],
                                                   [1, 128]])
                                    ph.raw("tensor", lambda e, lhs=lhs, rhs=rhs,
                                           sg=sg, st=(j == 0), sp=(j == nch - 1):
                                           e.matmul(
                                        cop[:, sg * 512:(sg + 1) * 512], lhs, rhs,
                                        start=st, stop=sp))
                            pm_cx = ph.inc_last("tensor", sPE)
                            ph.wait("vector", sPE, pm_cx)
                            cop_free = ph.op(
                                "vector", lambda e, half=half: e.tensor_copy(
                                    ctxT[:, half * 1024:(half + 1) * 1024],
                                    cop[:, 0:1024]), (sDV, 1))
                            if half == 0:
                                ph.wait("tensor", sDV, cop_free)
                        ph.wait("tensor", sDV, cop_free)
                        ph.wait("tensor", sLo, oload)
                        ph.wait("vector", sLx[xb], x_ready)
                        ph.wait("vector", sSx)
                        ph.wait("vector", sSx1)
                        dv_y2 = 0
                        for h2 in range(2):
                            for c in range(16):
                                r = 2 * (c % 8) + (c // 8)
                                for sg in range(2):
                                    ph.raw("tensor", lambda e, c=c, r=r, sg=sg,
                                           h2=h2, st=(c == 0), sp=(c == 15):
                                           e.matmul(
                                        cop[:, sg * 512:(sg + 1) * 512],
                                        ctxT[:, c * 128:(c + 1) * 128],
                                        oTw[:, r * D + h2 * 1024 + sg * 512:
                                            r * D + h2 * 1024 + (sg + 1) * 512],
                                        start=st, stop=sp))
                            pm_o = ph.inc_last("tensor", sPE)
                            ph.wait("vector", sPE, pm_o)
                            dv_y2 = ph.op(
                                "vector", lambda e, h2=h2, xb=xb: e.tensor_tensor(
                                    y_sb[:, h2 * 1024:(h2 + 1) * 1024],
                                    cop[:, 0:1024],
                                    x_sb[xb][:, h2 * 1024:(h2 + 1) * 1024],
                                    ALU.add), (sDV, 1))
                            cop_free = dv_y2
                            if h2 == 0:
                                ph.wait("tensor", sDV, cop_free)
                        xfree[it] = dv_y2
                        ph.wait("scalar", sDV, dv_y2)
                        ac_sq = ph.op("scalar", lambda e: e.activation(
                            sqga[:, :], y_sb[:, :], AF.Square, accum_out=SC(39)),
                            (sAC, 1))
                        ph.raw("vector", lambda e: e.reduce_sum(SC(40), y_sb[:, :],
                                                                axis=AX.X))
                        ph.raw("vector", lambda e: e.tensor_scalar(
                            SC(40), SC(40), 1.0 / D, None, ALU.mult))
                        ph.raw("vector", lambda e: e.tensor_tensor(
                            SC(41), SC(40), SC(40), ALU.mult))
                        ph.wait("vector", sAC, ac_sq)
                        dv_var = ph.op("vector", lambda e: e.scalar_tensor_tensor(
                            SC(42), SC(39), 1.0 / D, SC(41), ALU.mult, ALU.subtract),
                            (sDV, 1))
                        ph.wait("scalar", sDV, dv_var)
                        ph.wait("scalar", sLc0, consts_ready)
                        ph.raw("scalar", lambda e: e.activation(
                            SC(43), SC(42), AF.Ln, bias=CST(1)))
                        ph.raw("scalar", lambda e: e.activation(
                            SC(43), SC(43), AF.Identity, scale=-0.5))
                        ac_rstd = ph.op("scalar", lambda e: e.activation(
                            SC(43), SC(43), AF.Exp), (sAC, 1))
                        ph.wait("vector", sAC, ac_rstd)
                        ph.raw("vector", lambda e: e.tensor_scalar(
                            y_sb[:, :], y_sb[:, :], SC(40), SC(43),
                            ALU.subtract, ALU.mult))
                        ph.raw("vector", lambda e: e.reduce_max(
                            SC(44), y_sb[:, :], axis=AX.X,
                            apply_absolute_value=True))
                        ph.raw("vector", lambda e: e.tensor_scalar(
                            SC(45), SC(44), 1.0 / 127.0, 1e-8, ALU.mult, ALU.max))
                        dv_q = ph.op("vector", lambda e: e.reciprocal(
                            SC(46), SC(45)), (sDV, 1))
                        ph.wait("scalar", sDV, dv_q)
                        ph.wait("scalar", sSx)
                        ph.raw("scalar", lambda e: e.activation(
                            i32a[:, :], y_sb[:, :], AF.Identity, scale=SC(46)))
                        epi = ph.op("scalar", lambda e: e.activation(
                            qx_sb[:, :], i32a[:, :], AF.Identity, scale=SC(45)),
                            (sAC, 1))
                        ph.wait("scalar", sAC, epi)
                        ph.wait("gpsimd", sAC, epi)
                        ph.dma("gpsimd", x1_d[trow:trow + TT, :], y_sb[:, :], sSx1)
                        ph.dma("scalar", qx_d[trow:trow + TT, :], qx_sb[:, :], sSx)
                        sx_done[it] = (sSx.n, sSx1.n)
                    qT_free = last_sc_pm
                kT_free = last_sc_pm
            ph.flush()
            esA2.close()
        esAT.close()

        if "B" in phases:
            do_C = "C" in phases
            a2_pe, a2_dv, a2_ac = sPE.n, sDV.n, sAC.n
            NJ1 = 2 * I // 512
            HTOK = TL // 2
            HT = HTOK // TT
            IH = I // 2
            NCH = IH // 128
            NW2 = 2 * (D // 128)
            esB = ExitStack()
            qxTh = esB.enter_context(nc.sbuf_tensor("qxTh", [128, NC * HTOK], BF16))
            qxt_in = [esB.enter_context(nc.sbuf_tensor(f"qxt{b}", [128, D], BF16))
                      for b in range(2)]
            wst = [esB.enter_context(nc.sbuf_tensor(f"wst{b}", [128, D], F32))
                   for b in range(4)]
            i32w = esB.enter_context(nc.sbuf_tensor("i32w", [128, D], I32))
            qwb = [esB.enter_context(nc.sbuf_tensor(f"qwb{b}", [128, D], BF16))
                   for b in range(4)]
            w1T = [esB.enter_context(nc.sbuf_tensor(f"w1T{b}", [128, NC * 512], BF16))
                   for b in range(2)]
            silu_t = [esB.enter_context(nc.sbuf_tensor(f"silu{b}", [128, 256], F32))
                      for b in range(4)]
            hst = [esB.enter_context(nc.sbuf_tensor(f"hst{b}", [128, 256], BF16))
                   for b in range(6)]
            wst2 = esB.enter_context(nc.sbuf_tensor("w2st", [128, IH], F32))
            qw2b = esB.enter_context(nc.sbuf_tensor("qw2b", [128, IH], BF16))
            stw2 = esB.enter_context(nc.sbuf_tensor("stw2", [128, IH], BF16))
            ps = [esB.enter_context(nc.psum_tensor(f"ps{b}", [128, 512], F32))
                  for b in range(4)]
            trps = [esB.enter_context(nc.psum_tensor(f"trps{b}", [128, 1024], BF16))
                    for b in range(4)]
            sLq = Sem(es, nc, "sLq", "sync")
            sLw1j = Sem(es, nc, "sLw1j", "sync")
            sLw2j = Sem(es, nc, "sLw2j", "gpsimd")
            sShS = Sem(es, nc, "sShS", "scalar")
            sShG = Sem(es, nc, "sShG", "gpsimd")
            sSw2 = Sem(es, nc, "sSw2", "scalar")
            for q in ("sync", "scalar", "gpsimd"):
                ph.wait(q, sPE, a2_pe)
                ph.wait(q, sDV, a2_dv)
                ph.wait(q, sAC, a2_ac)

            trp_free = {b: 0 for b in range(4)}
            qx_free = {}
            hmul_done = {}
            tr_done_w1 = {}
            round_done = {}
            dvq_done = {}
            deq_done = {}
            wrd = {}
            evac_last = {}
            hst_store = {}
            st = {"hrot": 0, "trot": 0, "w2rot": 0, "w2i": 0,
                  "w2_round_done": 0, "w2tr_done": 0, "stw2_store": 0,
                  "w2dv": 0, "w2ld": 0}

            def w1_loads(jg):
                jc_ = jg % NJ1
                rows = [jc_ * 256, jc_ * 256 + 128,
                        I + jc_ * 256, I + jc_ * 256 + 128]
                if jg >= 1:
                    ph.wait("sync", sAC, round_done[jg - 1])
                for s in range(4):
                    ph.dma("sync", wst[s][:, :],
                           w1_h[rows[s]:rows[s] + 128, :], sLw1j)
                wrd[jg] = sLw1j.n

            def w1_dve(jg, hset):
                ph.wait("vector", sLw1j, wrd[jg])
                for s in (2 * hset, 2 * hset + 1):
                    cq = 4 + 3 * s
                    ph.raw("vector", lambda e, s=s, cq=cq: e.reduce_max(
                        SC(cq), wst[s][:, :], axis=AX.X,
                        apply_absolute_value=True))
                    ph.raw("vector", lambda e, cq=cq: e.tensor_scalar(
                        SC(cq + 1), SC(cq), 1.0 / 127.0, 1e-8, ALU.mult, ALU.max))
                    ph.raw("vector", lambda e, cq=cq: e.reciprocal(
                        SC(cq + 2), SC(cq + 1)))
                dvq_done[(jg, hset)] = ph.mark("vector", sDV)

            def w1_round(jg, s):
                cq = 4 + 3 * s
                ph.wait("scalar", sDV, dvq_done[(jg, s // 2)])
                if jg >= 1 and s == 0:
                    ph.wait("scalar", sPE, tr_done_w1[jg - 1])
                ph.raw("scalar", lambda e, s=s, cq=cq: e.activation(
                    i32w[:, :], wst[s][:, :], AF.Identity, scale=SC(cq + 2)))
                ph.raw("scalar", lambda e, s=s, cq=cq: e.activation(
                    qwb[s][:, :], i32w[:, :], AF.Identity, scale=SC(cq + 1)))
                m = ph.mark("scalar", sAC)
                deq_done[(jg, s)] = m
                if s == 3:
                    round_done[jg] = m

            def w1_trgroup(jg, g):
                s, gi = g // 2, g % 2
                jp = jg % 2
                tb = st["trot"] % 4
                st["trot"] += 1
                ph.wait("tensor", sAC, deq_done[(jg, s)])
                ph.wait("tensor", sDV, trp_free[tb])
                for c in range(gi * 8, gi * 8 + 8):
                    ph.raw("tensor", lambda e, c=c, gi=gi, s=s, tb=tb: e.transpose(
                        trps[tb][:, (c - gi * 8) * 128:(c - gi * 8 + 1) * 128],
                        qwb[s][:, c * 128:(c + 1) * 128], id16[:, :]))
                pm = ph.inc_last("tensor", sPE)
                tr_done_w1[jg] = pm
                ph.wait("vector", sPE, pm)
                ev = ph.op("vector", lambda e, tb=tb, gi=gi, s=s, jp=jp:
                           e.tensor_copy(
                    bass.AP(w1T[jp], (gi * 8) * 512 + s * 128,
                            [[NC * 512, 128], [512, 8], [1, 128]]),
                    trps[tb][:, 0:1024]), (sDV, 1))
                trp_free[tb] = ev
                evac_last[jg] = ev

            def w2_load():
                w2i = st["w2i"]
                s2, ihh = w2i // 2, w2i % 2
                if st["w2_round_done"]:
                    ph.wait("gpsimd", sAC, st["w2_round_done"])
                ph.dma("gpsimd", wst2[:, :],
                       w2_h[s2 * 128:(s2 + 1) * 128, ihh * IH:(ihh + 1) * IH],
                       sLw2j)
                st["w2ld"] = sLw2j.n

            def w2_dve():
                ph.wait("vector", sLw2j, st["w2ld"])
                ph.raw("vector", lambda e: e.reduce_max(
                    SC(16), wst2[:, :], axis=AX.X, apply_absolute_value=True))
                ph.raw("vector", lambda e: e.tensor_scalar(
                    SC(17), SC(16), 1.0 / 127.0, 1e-8, ALU.mult, ALU.max))
                ph.raw("vector", lambda e: e.reciprocal(SC(18), SC(17)))
                st["w2dv"] = ph.mark("vector", sDV)

            def w2_round(pas):
                f0 = pas * 2048
                f1 = min(f0 + 2048, IH)
                ph.wait("scalar", sDV, st["w2dv"])
                if pas == 0 and st["w2tr_done"]:
                    ph.wait("scalar", sPE, st["w2tr_done"])
                ph.raw("scalar", lambda e, f0=f0, f1=f1: e.activation(
                    i32w[:, 0:f1 - f0], wst2[:, f0:f1], AF.Identity, scale=SC(18)))
                ph.raw("scalar", lambda e, f0=f0, f1=f1: e.activation(
                    qw2b[:, f0:f1], i32w[:, 0:f1 - f0], AF.Identity, scale=SC(17)))
                if f1 == IH:
                    st["w2_round_done"] = ph.mark("scalar", sAC)

            def w2_transposes():
                w2i = st["w2i"]
                s2, ihh = w2i // 2, w2i % 2
                ph.wait("tensor", sAC, st["w2_round_done"])
                ev2 = 0
                pm2 = 0
                for g0 in range(0, NCH, 8):
                    g1 = min(g0 + 8, NCH)
                    tb = st["trot"] % 4
                    st["trot"] += 1
                    ph.wait("tensor", sDV, trp_free[tb])
                    for c in range(g0, g1):
                        ph.raw("tensor", lambda e, c=c, g0=g0, tb=tb: e.transpose(
                            trps[tb][:, (c - g0) * 128:(c - g0 + 1) * 128],
                            qw2b[:, c * 128:(c + 1) * 128], id16[:, :]))
                    pm2 = ph.inc_last("tensor", sPE)
                    ph.wait("vector", sPE, pm2)
                    if st["stw2_store"]:
                        ph.wait("vector", sSw2, st["stw2_store"])
                    ev2 = ph.op("vector", lambda e, tb=tb, g0=g0, g1=g1:
                                e.tensor_copy(
                        stw2[:, g0 * 128:g1 * 128],
                        trps[tb][:, 0:(g1 - g0) * 128]), (sDV, 1))
                    trp_free[tb] = ev2
                st["w2tr_done"] = pm2
                ph.wait("scalar", sDV, ev2)
                ph.dma("scalar", qw2R_d[s2 * 128:(s2 + 1) * 128,
                                        ihh * IH:(ihh + 1) * IH],
                       stw2[:, :], sSw2)
                st["stw2_store"] = sSw2.n
                st["w2i"] = w2i + 1

            for half in range(2):
                for t in range(HT):
                    ht = half * HT + t
                    tb2 = t % 2
                    if ht >= 2:
                        ph.wait("sync", sDV, qx_free[ht - 2])
                    if sx_done:
                        ph.wait("sync", sSx, sx_done[ht][0])
                    ph.dma("sync", qxt_in[tb2][:, :],
                           qx_d[ht * TT:(ht + 1) * TT, :], sLq)
                    lq = sLq.n
                    ph.wait("tensor", sLq, lq)
                    dv = 0
                    for g in range(2):
                        tb = st["trot"] % 4
                        st["trot"] += 1
                        ph.wait("tensor", sDV, trp_free[tb])
                        for c in range(g * 8, g * 8 + 8):
                            ph.raw("tensor", lambda e, c=c, g=g, tb2=tb2, tb=tb:
                                   e.transpose(
                                trps[tb][:, (c - g * 8) * 128:(c - g * 8 + 1) * 128],
                                qxt_in[tb2][:, c * 128:(c + 1) * 128], id16[:, :]))
                        pm = ph.inc_last("tensor", sPE)
                        ph.wait("vector", sPE, pm)
                        dv = ph.op("vector", lambda e, tb=tb, g=g, t=t: e.tensor_copy(
                            bass.AP(qxTh, (g * 8) * HTOK + t * 128,
                                    [[NC * HTOK, 128], [HTOK, 8], [1, 128]]),
                            trps[tb][:, 0:1024]), (sDV, 1))
                        trp_free[tb] = dv
                    qx_free[ht] = dv
                jg0 = half * NJ1
                ph.wait("tensor", sLc0, consts_ready)
                w1_loads(jg0)
                w1_dve(jg0, 0)
                w1_dve(jg0, 1)
                for s in range(4):
                    w1_round(jg0, s)
                for g in range(8):
                    w1_trgroup(jg0, g)
                for jc in range(NJ1):
                    jg = half * NJ1 + jc
                    jcp = jc % 2
                    nxt = jg + 1 if jc + 1 < NJ1 else None
                    for t in range(HT):
                        pp = t % 4
                        key = (jg, t - 4) if t >= 4 else (jg - 1, HT - 4 + t)
                        ph.wait("tensor", sDV, evac_last[jg])
                        ph.wait("tensor", sDV, hmul_done.get(key, 0))
                        for c in range(NC):
                            ph.raw("tensor", lambda e, c=c, t=t, pp=pp, jcp=jcp:
                                   e.matmul(
                                ps[pp][:, 0:512],
                                qxTh[:, c * HTOK + t * 128: c * HTOK + (t + 1) * 128],
                                w1T[jcp][:, c * 512:(c + 1) * 512],
                                start=(c == 0), stop=(c == NC - 1)))
                        pmm = ph.inc_last("tensor", sPE)
                        if nxt is not None and 7 <= t <= 14:
                            w1_trgroup(nxt, t - 7)
                        if t == 12 and st["w2i"] < NW2 and jg >= st["w2i"]:
                            w2_transposes()
                        if t == 0:
                            if nxt is not None:
                                w1_loads(nxt)
                            if st["w2i"] < NW2 and jg >= st["w2i"]:
                                w2_load()
                        if nxt is not None and 3 <= t <= 6:
                            w1_round(nxt, t - 3)
                        if st["w2i"] < NW2 and jg >= st["w2i"] and t in (9, 10):
                            w2_round(t - 9)
                        ph.wait("scalar", sPE, pmm)
                        ph.wait("scalar", sDV, hmul_done.get(key, 0))
                        ac_si = ph.op("scalar", lambda e, pp=pp: e.activation(
                            silu_t[pp][:, :], ps[pp][:, 0:256], AF.Silu), (sAC, 1))
                        if nxt is not None and t in (1, 2):
                            w1_dve(nxt, t - 1)
                        if st["w2i"] < NW2 and jg >= st["w2i"] and t == 3:
                            w2_dve()
                        r = st["hrot"] % 6
                        st["hrot"] += 1
                        ph.wait("vector", sAC, ac_si)
                        if r in hst_store:
                            sem_, cnt_ = hst_store[r]
                            ph.wait("vector", sem_, cnt_)
                        hm = ph.op("vector", lambda e, pp=pp, r=r: e.tensor_tensor(
                            hst[r][:, :], silu_t[pp][:, :], ps[pp][:, 256:512],
                            ALU.mult), (sDV, 1))
                        hmul_done[(jg, t)] = hm
                        qn = "scalar" if r % 2 == 0 else "gpsimd"
                        sem_ = sShS if r % 2 == 0 else sShG
                        ph.wait(qn, sDV, hm)
                        rb = half * HTOK + t * TT
                        ph.dma(qn, h_d[rb:rb + TT, jc * 256:(jc + 1) * 256],
                               hst[r][:, :], sem_)
                        hst_store[r] = (sem_, sem_.n)
            st1_pe, st1_dv, st1_ac = sPE.n, sDV.n, sAC.n
            sh_s, sh_g, sw2_n = sShS.n, sShG.n, sSw2.n
            ph.flush()
            esB.close()

            TB2 = 1024
            NB2 = TL // TB2
            NT2 = TB2 // TT
            esB2 = ExitStack()
            hqT = esB2.enter_context(nc.sbuf_tensor("hqT", [128, NCI * TB2], BF16))
            w2Ts = [esB2.enter_context(nc.sbuf_tensor(f"w2Ts{b}", [128, NCI * 128], BF16))
                    for b in range(2)]
            hrow = esB2.enter_context(nc.sbuf_tensor("hrow", [128, I], BF16))
            hq = [esB2.enter_context(nc.sbuf_tensor(f"hq{b}", [128, I], BF16))
                  for b in range(2)]
            i32b = esB2.enter_context(nc.sbuf_tensor("i32b", [128, 1024], I32))
            accst = [esB2.enter_context(nc.sbuf_tensor(f"accst{b}", [128, TB2], BF16))
                     for b in range(2)]
            trp2 = [esB2.enter_context(nc.psum_tensor(f"trp2{b}", [128, 1024], BF16))
                    for b in range(4)]
            fps = [esB2.enter_context(nc.psum_tensor(f"fps{b}", [128, TB2], F32))
                   for b in range(2)]
            sLh = Sem(es, nc, "sLh", "gpsimd")
            sLw2s = Sem(es, nc, "sLw2s", "sync")
            sSacc = [Sem(es, nc, f"sSacc{b}", "scalar") for b in range(2)]
            if do_C:
                acc_c = [esB2.enter_context(nc.sbuf_tensor(f"acc_c{b}", [128, D], BF16))
                         for b in range(2)]
                x1c = [esB2.enter_context(nc.sbuf_tensor(f"x1c{b}", [128, D], BF16))
                       for b in range(2)]
                yc = esB2.enter_context(nc.sbuf_tensor("yc", [128, D], F32))
                yo = esB2.enter_context(nc.sbuf_tensor("yo", [128, D], F32))
                sLcc = [Sem(es, nc, f"sLcc{b}", "scalar") for b in range(2)]
                sLx1c = [Sem(es, nc, f"sLx1c{b}", "gpsimd") for b in range(2)]
                sSy = Sem(es, nc, "sSy", "scalar")
            for q in ("sync", "scalar", "gpsimd"):
                ph.wait(q, sPE, st1_pe)
                ph.wait(q, sDV, st1_dv)
                ph.wait(q, sAC, st1_ac)
            ph.wait("gpsimd", sShS, sh_s)
            ph.wait("sync", sSw2, sw2_n)

            trp2_free = {b: 0 for b in range(4)}
            t2rot = 0
            hq_tr_done = {0: 0, 1: 0}
            hq_round = {}
            down_ocu = {}
            fps_ev = {}
            add_done = {}
            sq_doneC = {}
            sacc_counts = {}

            def emit_C_tile(blk, k2):
                sc0, sc1 = sacc_counts[blk]
                ph.wait("scalar", sSacc[0], sc0)
                ph.wait("scalar", sSacc[1], sc1)
                if True:
                    it = blk * NT2 + k2
                    b = it % 2
                    trow = it * TT
                    if it >= 2:
                        ph.wait("gpsimd", sAC, sq_doneC[it - 2])
                        ph.wait("scalar", sAC, sq_doneC[it - 2])
                    if sx_done:
                        ph.wait("gpsimd", sSx1, sx_done[it][1])
                    ph.dma("scalar", acc_c[b][:, :], accT_d[:, trow:trow + TT],
                           sLcc[b], transpose=True)
                    ph.dma("gpsimd", x1c[b][:, :], x1_d[trow:trow + TT, :],
                           sLx1c[b])
                    ph.wait("vector", sLcc[b])
                    ph.wait("vector", sLx1c[b])
                    ph.wait("vector", sSy)
                    add_done[it] = ph.op("vector", lambda e, b=b: e.tensor_tensor(
                        yc[:, :], acc_c[b][:, :], x1c[b][:, :], ALU.add), (sDV, 1))
                    ph.wait("scalar", sDV, add_done[it])
                    ac_sq = ph.op("scalar", lambda e, b=b: e.activation(
                        acc_c[b][:, :], yc[:, :], AF.Square, accum_out=SC(7)),
                        (sAC, 1))
                    sq_doneC[it] = ac_sq
                    ph.raw("vector", lambda e: e.reduce_sum(SC(8), yc[:, :],
                                                            axis=AX.X))
                    ph.raw("vector", lambda e: e.tensor_scalar(
                        SC(8), SC(8), 1.0 / D, None, ALU.mult))
                    ph.raw("vector", lambda e: e.tensor_tensor(SC(9), SC(8), SC(8),
                                                               ALU.mult))
                    ph.wait("vector", sAC, ac_sq)
                    dv_var = ph.op("vector", lambda e: e.scalar_tensor_tensor(
                        SC(10), SC(7), 1.0 / D, SC(9), ALU.mult, ALU.subtract),
                        (sDV, 1))
                    ph.wait("scalar", sDV, dv_var)
                    ph.wait("scalar", sLc0, consts_ready)
                    ph.raw("scalar", lambda e: e.activation(SC(11), SC(10), AF.Ln,
                                                            bias=CST(1)))
                    ph.raw("scalar", lambda e: e.activation(
                        SC(11), SC(11), AF.Identity, scale=-0.5))
                    ac_r = ph.op("scalar", lambda e: e.activation(
                        SC(11), SC(11), AF.Exp), (sAC, 1))
                    ph.wait("vector", sAC, ac_r)
                    dv_y = ph.op("vector", lambda e: e.tensor_scalar(
                        yo[:, :], yc[:, :], SC(8), SC(11), ALU.subtract, ALU.mult),
                        (sDV, 1))
                    ph.wait("scalar", sDV, dv_y)
                    ph.dma("scalar", y_out[trow:trow + TT, :], yo[:, :], sSy)

            for b2 in range(NB2):
                t0 = b2 * TB2
                ev_last = 0
                for k2 in range(NT2):
                    it = b2 * NT2 + k2
                    hb = it % 2
                    cqh = 19 + 3 * (it % 2)
                    if it >= 1:
                        ph.wait("gpsimd", sAC, hq_round[it - 1])
                    ph.dma("gpsimd", hrow[:, :],
                           h_d[it * TT:(it + 1) * TT, :], sLh)
                    hr = sLh.n
                    ph.wait("vector", sLh, hr)
                    ph.raw("vector", lambda e, cqh=cqh: e.reduce_max(
                        SC(cqh), hrow[:, :], axis=AX.X,
                        apply_absolute_value=True))
                    ph.raw("vector", lambda e, cqh=cqh: e.tensor_scalar(
                        SC(cqh + 1), SC(cqh), 1.0 / 127.0, 1e-8, ALU.mult, ALU.max))
                    ph.raw("vector", lambda e, cqh=cqh: e.reciprocal(
                        SC(cqh + 2), SC(cqh + 1)))
                    hdv = ph.mark("vector", sDV)
                    ph.wait("scalar", sDV, hdv)
                    if hq_tr_done[hb]:
                        ph.wait("scalar", sPE, hq_tr_done[hb])
                    for f0 in range(0, I, 1024):
                        f1 = min(f0 + 1024, I)
                        ph.raw("scalar", lambda e, hb=hb, f0=f0, f1=f1, cqh=cqh:
                               e.activation(i32b[:, 0:f1 - f0], hrow[:, f0:f1],
                                            AF.Identity, scale=SC(cqh + 2)))
                        ph.raw("scalar", lambda e, hb=hb, f0=f0, f1=f1, cqh=cqh:
                               e.activation(hq[hb][:, f0:f1], i32b[:, 0:f1 - f0],
                                            AF.Identity, scale=SC(cqh + 1)))
                    hq_round[it] = ph.mark("scalar", sAC)
                    ph.wait("tensor", sAC, hq_round[it])
                    for g0 in range(0, NCI, 8):
                        g1 = min(g0 + 8, NCI)
                        tb = t2rot % 4
                        t2rot += 1
                        ph.wait("tensor", sDV, trp2_free[tb])
                        for c in range(g0, g1):
                            ph.raw("tensor", lambda e, c=c, g0=g0, tb=tb, hb=hb:
                                   e.transpose(
                                trp2[tb][:, (c - g0) * 128:(c - g0 + 1) * 128],
                                hq[hb][:, c * 128:(c + 1) * 128], id16[:, :]))
                        pmt = ph.inc_last("tensor", sPE)
                        ph.wait("vector", sPE, pmt)
                        ev_last = ph.op("vector",
                                        lambda e, tb=tb, g0=g0, g1=g1, k2=k2:
                                        e.tensor_copy(
                            bass.AP(hqT, g0 * TB2 + k2 * 128,
                                    [[NCI * TB2, 128], [TB2, g1 - g0], [1, 128]]),
                            trp2[tb][:, 0:(g1 - g0) * 128]), (sDV, 1))
                        trp2_free[tb] = ev_last
                    hq_tr_done[hb] = pmt
                hq_ready = ev_last
                for ocu in range(16):
                    wb = ocu % 2
                    guard = down_ocu.get((b2, ocu - 2),
                                         down_ocu.get((b2 - 1, 14 + ocu), 0))
                    ph.wait("sync", sPE, guard)
                    ph.dma("sync", w2Ts[wb][:, :],
                           qw2R_d[ocu * 128:(ocu + 1) * 128, :], sLw2s)
                    wr = sLw2s.n
                    ph.wait("tensor", sLw2s, wr)
                    ph.wait("tensor", sDV, hq_ready)
                    ph.wait("tensor", sDV, fps_ev.get((b2, ocu - 2),
                                                      fps_ev.get((b2 - 1, 14 + ocu), 0)))
                    for c2 in range(NCI):
                        for seg in range(2):
                            ph.raw("tensor", lambda e, c2=c2, seg=seg, wb=wb:
                                   e.matmul(
                                fps[wb][:, seg * 512:(seg + 1) * 512],
                                w2Ts[wb][:, c2 * 128:(c2 + 1) * 128],
                                hqT[:, c2 * TB2 + seg * 512: c2 * TB2 + (seg + 1) * 512],
                                start=(c2 == 0), stop=(c2 == NCI - 1)))
                    pmd = ph.inc_last("tensor", sPE)
                    down_ocu[(b2, ocu)] = pmd
                    ph.wait("vector", sPE, pmd)
                    ph.wait("vector", sSacc[wb])
                    ev = ph.op("vector", lambda e, wb=wb: e.tensor_copy(
                        accst[wb][:, :], fps[wb][:, 0:TB2]), (sDV, 1))
                    fps_ev[(b2, ocu)] = ev
                    ph.wait("scalar", sDV, ev)
                    ph.dma("scalar", accT_d[ocu * 128:(ocu + 1) * 128, t0:t0 + TB2],
                           accst[wb][:, :], sSacc[wb])
                    if do_C and b2 >= 1 and ocu % 2 == 1:
                        emit_C_tile(b2 - 1, ocu // 2)
                sacc_counts[b2] = (sSacc[0].n, sSacc[1].n)
            if do_C:
                for k2 in range(NT2):
                    emit_C_tile(NB2 - 1, k2)
            ph.flush()
            esB2.close()

    return nc


def _host_arrays(q_proj, k_proj, v_proj, o_proj, q_norm, k_norm, w_gate_up, w_down):
    f32 = np.float32
    arrs = {}
    arrs["qTw"] = np.ascontiguousarray(np.asarray(q_proj, f32).T)
    arrs["kTw"] = np.ascontiguousarray(np.asarray(k_proj, f32).T)
    arrs["vTw"] = np.ascontiguousarray(np.asarray(v_proj, f32).T)
    arrs["oTw"] = np.ascontiguousarray(np.asarray(o_proj, f32).T)
    arrs["w1n"] = np.ascontiguousarray(np.asarray(w_gate_up, f32))
    arrs["w2n"] = np.ascontiguousarray(np.asarray(w_down, f32))
    arrs["qnb"] = np.ascontiguousarray(
        np.broadcast_to(np.tile(np.asarray(q_norm, f32), NH)[None, :], (128, D)))
    arrs["knb"] = np.ascontiguousarray(
        np.broadcast_to(np.asarray(k_norm, f32)[None, :], (128, HD)))
    hd = np.float32(HD)
    idx = np.arange(128, dtype=f32)
    inv = (1.0 / np.power(np.float32(1000000.0), (2.0 * idx / hd), dtype=f32)).astype(f32)
    pos = np.arange(L, dtype=f32)
    fr = pos[:, None] * inv[None, :]
    c = np.cos(fr).astype(f32)
    s = np.sin(fr).astype(f32)
    arrs["ctab"] = np.ascontiguousarray(np.concatenate([c, c], axis=1))
    arrs["stab"] = np.ascontiguousarray(np.concatenate([-s, s], axis=1))
    m = np.zeros((128, 128), f32)
    m[np.triu_indices(128, 1)] = NEGBIG
    arrs["mskd"] = m
    arrs["id16"] = np.eye(128, dtype=BF)
    cstv = np.zeros((128, 8), f32)
    cstv[:, 0] = 1e-6
    cstv[:, 1] = 1e-5
    arrs["cst"] = cstv
    return arrs


def kernel(L0_attn_x, L0_attn_q_proj, L0_attn_k_proj, L0_attn_v_proj, L0_attn_o_proj,
           L0_attn_q_norm, L0_attn_k_norm, L0_ffn_w_gate_up, L0_ffn_w_down):
    n = 8
    x = np.asarray(L0_attn_x, np.float32)
    B, Lq, Dm = x.shape
    shared = _host_arrays(L0_attn_q_proj, L0_attn_k_proj, L0_attn_v_proj, L0_attn_o_proj,
                          L0_attn_q_norm, L0_attn_k_norm, L0_ffn_w_gate_up, L0_ffn_w_down)
    nc = build()
    in_maps = []
    for c in range(n):
        m = dict(shared)
        m["x_in"] = np.ascontiguousarray(x[2 * c:2 * c + 2].reshape(SEQ * L, D))
        in_maps.append(m)
    res = run_bass_kernel_spmd(nc, in_maps, core_ids=list(range(n)))
    out = np.empty((B, Lq, Dm), np.float32)
    for c in range(n):
        out[2 * c:2 * c + 2] = res.results[c]["y"].reshape(SEQ, L, D)
    return out
